# version 01
# baseline (speedup 1.0000x reference)
import contextlib
import numpy as np
import concourse.bass as bass
import concourse.mybir as mybir
from concourse.bass_utils import run_bass_kernel_spmd

F32 = mybir.dt.float32
BF16 = mybir.dt.bfloat16
AF = mybir.ActivationFunctionType
ALU = mybir.AluOpType

NCORES = 8
T = 2048
DM = 2048
H = 16
KT = DM // 128
FF = 8192
TT = 512
NTILE = T // TT
EPS_LN = 1e-5
EPS_DEN = 1e-6
INV_DM = 1.0 / DM
DBG_TILE = 0
UP_USES_GATE = False


class Builder:
    def __init__(self, nc):
        self.nc = nc
        self.ops = {"sync": [], "scalar": [], "vector": [], "tensor": [], "gpsimd": []}
        self.cnt = {}
        self.sems = {}
        self.bank_guard = {}

    def emit(self, eng, fn):
        self.ops[eng].append(fn)

    def wait(self, eng, semname, val):
        if val is None or val <= 0:
            return
        h = self.sems[semname]
        self.emit(eng, lambda e, h=h, v=val: e.wait_ge(h, v))

    def inc_of(self, semname, n=1):
        self.cnt[semname] = self.cnt.get(semname, 0) + n
        return self.cnt[semname]

    def op(self, eng, build, inc=None):
        if inc is not None:
            semname, n = inc
            val = self.inc_of(semname, n)
            h = self.sems[semname]
            self.emit(eng, lambda e, b=build, h=h, n=n: b(e).then_inc(h, n))
            return val
        self.emit(eng, lambda e, b=build: b(e))
        return None

    def guard_bank(self, bank, eng="tensor"):
        g = self.bank_guard.get(bank)
        if g is not None:
            self.wait(eng, g[0], g[1])

    def set_guard(self, bank, semname, val):
        self.bank_guard[bank] = (semname, val)


def build_program(debug=False, dbg_tile=0):
    nc = bass.Bass("TRN2", target_bir_lowering=False)

    xT = nc.dram_tensor("xT", [DM, T], F32, kind="ExternalInput").ap()
    qkvW = nc.dram_tensor("qkvW", [H, 3, 128, KT * 128], F32, kind="ExternalInput").ap()
    outW = nc.dram_tensor("outW", [16, 128, KT * 128], F32, kind="ExternalInput").ap()
    gateW = nc.dram_tensor("gateW", [8, KT, 128, 1024], F32, kind="ExternalInput").ap()
    upW = nc.dram_tensor("upW", [8, KT, 128, 1024], F32, kind="ExternalInput").ap()
    downW = nc.dram_tensor("downW", [8, 2, 8, 128, 1024], F32, kind="ExternalInput").ap()
    dk = nc.dram_tensor("dk", [1, H], F32, kind="ExternalInput").ap()
    y = nc.dram_tensor("y", [DM, T], F32, kind="ExternalOutput").ap()

    s_qkv = nc.dram_tensor("s_qkv", [H, 3, 2, 128, KT * 128], BF16).ap()
    s_out = nc.dram_tensor("s_out", [16, 2, 128, KT * 128], BF16).ap()
    s_gate = nc.dram_tensor("s_gate", [8, KT, 2, 128, 1024], BF16).ap()
    s_up = nc.dram_tensor("s_up", [8, KT, 2, 128, 1024], BF16).ap()
    s_down = nc.dram_tensor("s_down", [8, 2, 8, 2, 128, 1024], BF16).ap()

    dbg = {}
    if debug:
        dbg["xn_hi"] = nc.dram_tensor("d_xnhi", [128, KT, TT], BF16, kind="ExternalOutput").ap()
        dbg["q0"] = nc.dram_tensor("d_q0", [128, TT], F32, kind="ExternalOutput").ap()
        dbg["k0"] = nc.dram_tensor("d_k0", [128, TT], F32, kind="ExternalOutput").ap()
        dbg["kv0"] = nc.dram_tensor("d_kv0", [128, TT], F32, kind="ExternalOutput").ap()
        dbg["rkv0"] = nc.dram_tensor("d_rkv0", [128, TT], F32, kind="ExternalOutput").ap()
        dbg["rks0"] = nc.dram_tensor("d_rks0", [128, TT], F32, kind="ExternalOutput").ap()
        dbg["att_hi"] = nc.dram_tensor("d_atthi", [128, KT, TT], BF16, kind="ExternalOutput").ap()
        dbg["x2"] = nc.dram_tensor("d_x2", [128, KT, TT], F32, kind="ExternalOutput").ap()
        dbg["h0"] = nc.dram_tensor("d_h0", [128, 8, TT], BF16, kind="ExternalOutput").ap()
        dbg["gate0"] = nc.dram_tensor("d_gate0", [128, 8, TT], F32, kind="ExternalOutput").ap()
        dbg["h0lo"] = nc.dram_tensor("d_h0lo", [128, 8, TT], BF16, kind="ExternalOutput").ap()
        dbg["up0"] = nc.dram_tensor("d_up0", [128, 8, TT // 2], BF16, kind="ExternalOutput").ap()
        dbg["xn2_hi"] = nc.dram_tensor("d_xn2hi", [128, KT, TT], BF16, kind="ExternalOutput").ap()

    stack = contextlib.ExitStack()
    with stack:
        xm = stack.enter_context(nc.sbuf_tensor("xm", [128, KT, TT], F32))
        xnt = stack.enter_context(nc.sbuf_tensor("xnt", [128, 2, TT], F32))
        xnh = stack.enter_context(nc.sbuf_tensor("xnh", [128, KT, TT], BF16))
        xnl = stack.enter_context(nc.sbuf_tensor("xnl", [128, KT, TT], BF16))
        ath = stack.enter_context(nc.sbuf_tensor("ath", [128, KT, TT], BF16))
        atl = stack.enter_context(nc.sbuf_tensor("atl", [128, KT, TT], BF16))
        sq = stack.enter_context(nc.sbuf_tensor("sq", [128, 2, TT], F32))
        q_s = stack.enter_context(nc.sbuf_tensor("q_s", [128, TT], F32))
        k_s = stack.enter_context(nc.sbuf_tensor("k_s", [128, TT], F32))
        kv_s = stack.enter_context(nc.sbuf_tensor("kv_s", [128, TT], F32))
        rkv_s = stack.enter_context(nc.sbuf_tensor("rkv_s", [128, TT], F32))
        rks_s = stack.enter_context(nc.sbuf_tensor("rks_s", [128, TT], F32))
        u_s = stack.enter_context(nc.sbuf_tensor("u_s", [128, 2, TT], F32))
        w_s = stack.enter_context(nc.sbuf_tensor("w_s", [128, TT], F32))
        ex_s = stack.enter_context(nc.sbuf_tensor("ex_s", [128, TT], F32))
        of_s = stack.enter_context(nc.sbuf_tensor("of_s", [128, TT], F32))
        gate_s = stack.enter_context(nc.sbuf_tensor("gate_s", [128, 8, TT], F32))
        hf_s = stack.enter_context(nc.sbuf_tensor("hf_s", [128, TT], F32))
        stats = stack.enter_context(nc.sbuf_tensor("stats", [1, 5, TT], F32))
        rden_s = stack.enter_context(nc.sbuf_tensor("rden_s", [1, 2, TT], F32))
        mb_s = stack.enter_context(nc.sbuf_tensor("mb_s", [128, TT], F32))
        rb_s = stack.enter_context(nc.sbuf_tensor("rb_s", [128, TT], F32))
        wq_b = stack.enter_context(nc.sbuf_tensor("wq_b", [128, 2, 2, KT, 128], BF16))
        wo_b = wq_b
        wf_b = stack.enter_context(nc.sbuf_tensor("wf_b", [128, 2, 2, 1024], BF16))
        pf_b = stack.enter_context(nc.sbuf_tensor("pf_b", [128, 2, 1024], F32))
        ph_b = stack.enter_context(nc.sbuf_tensor("ph_b", [128, 2, 2, 1024], BF16))
        ones_c = stack.enter_context(nc.sbuf_tensor("ones_c", [128, 1], F32))
        ones_r = stack.enter_context(nc.sbuf_tensor("ones_r", [1, 128], F32))
        sig_s = stack.enter_context(nc.sbuf_tensor("sig_s", [1, H], F32))
        dec_s = stack.enter_context(nc.sbuf_tensor("dec_s", [128, H], F32))
        ckv_s = stack.enter_context(nc.sbuf_tensor("ckv_s", [128, H], F32))
        updbg = stack.enter_context(nc.sbuf_tensor("updbg", [128, 8, TT // 2], BF16)) if debug else None
        cks_s = stack.enter_context(nc.sbuf_tensor("cks_s", [128, H], F32))

        ps = stack.enter_context(nc.psum_tensor("ps", [128, 8, 512], F32))

        b = Builder(nc)
        for name in ("LD", "PRE", "YS", "TE", "VE", "SE", "GE"):
            b.sems[name] = stack.enter_context(nc.semaphore(name))
            b.cnt[name] = 0

        ge_init = None
        for t_, v_ in ((ones_c, 1.0), (ones_r, 1.0), (ckv_s, 0.0), (cks_s, 0.0)):
            ge_init = b.op("gpsimd", lambda e, t=t_, v=v_: e.memset(t[:, :], v), inc=("GE", 1))

        ld_dk = b.op("sync", lambda e: e.dma_start(sig_s[:, :], dk), inc=("LD", 16))
        b.wait("scalar", "LD", ld_dk)
        se_sig = b.op("scalar", lambda e: e.activation(sig_s[0:1, :], sig_s[0:1, :], AF.Sigmoid),
                      inc=("SE", 1))
        b.wait("tensor", "GE", ge_init)
        b.wait("tensor", "SE", se_sig)
        te_dec = b.op("tensor", lambda e: e.matmul(ps[:, 5, 0:H], ones_r[:, :], sig_s[0:1, :],
                                                   start=True, stop=True), inc=("TE", 1))
        b.wait("vector", "TE", te_dec)
        b.wait("vector", "GE", ge_init)
        ve_dec = b.op("vector", lambda e: e.tensor_copy(dec_s[:, :], ps[:, 5, 0:H]), inc=("VE", 1))
        b.set_guard(5, "VE", ve_dec)

        chunks = []
        for h in range(H):
            for t3 in range(3):
                for c in range(2):
                    chunks.append((("qkv", h, t3),
                                   qkvW[h, t3, :, c * 1024:(c + 1) * 1024],
                                   s_qkv[h, t3, 0, :, c * 1024:(c + 1) * 1024],
                                   s_qkv[h, t3, 1, :, c * 1024:(c + 1) * 1024]))
        for m in range(16):
            for c in range(2):
                chunks.append((("out", m),
                               outW[m, :, c * 1024:(c + 1) * 1024],
                               s_out[m, 0, :, c * 1024:(c + 1) * 1024],
                               s_out[m, 1, :, c * 1024:(c + 1) * 1024]))
        for hc in range(8):
            for kd in range(KT):
                chunks.append((("gate", hc, kd), gateW[hc, kd, :, :],
                               s_gate[hc, kd, 0, :, :], s_gate[hc, kd, 1, :, :]))
            for kd in range(KT):
                chunks.append((("up", hc, kd), upW[hc, kd, :, :],
                               s_up[hc, kd, 0, :, :], s_up[hc, kd, 1, :, :]))
            for dh in range(2):
                for kh in range(8):
                    chunks.append((("down", hc, dh, kh), downW[hc, dh, kh, :, :],
                                   s_down[hc, dh, kh, 0, :, :], s_down[hc, dh, kh, 1, :, :]))

        scr_done = {}
        pp = {"load_i": 0, "split_i": 0, "split_cap": {}, "write_cap": {}, "load_cap": {}}

        def pp_ensure_loads(upto):
            while pp["load_i"] <= min(upto, len(chunks) - 1):
                i = pp["load_i"]
                par = i % 2
                prev_split = pp["split_cap"].get(i - 2)
                if prev_split is not None:
                    b.wait("scalar", "VE", prev_split)
                key, src, dhi, dlo = chunks[i]
                pp["load_cap"][i] = b.op(
                    "scalar", lambda e, s=src, p=par: e.dma_start(pf_b[:, p, :], s),
                    inc=("PRE", 16))
                pp["load_i"] = i + 1

        def emit_prepass(n):
            for _ in range(n):
                i = pp["split_i"]
                if i >= len(chunks):
                    return
                pp_ensure_loads(i + 1)
                key, src, dhi, dlo = chunks[i]
                par = i % 2
                b.wait("vector", "PRE", pp["load_cap"][i])
                prev_wr = pp["write_cap"].get(i - 2)
                if prev_wr is not None:
                    b.wait("vector", "PRE", prev_wr)
                b.op("vector", lambda e, p=par: e.tensor_copy(ph_b[:, p, 0, :], pf_b[:, p, :]))
                sp = b.op("vector", lambda e, p=par: e.tensor_sub(
                    ph_b[:, p, 1, :], pf_b[:, p, :], ph_b[:, p, 0, :]), inc=("VE", 1))
                pp["split_cap"][i] = sp
                b.wait("scalar", "VE", sp)
                b.op("scalar", lambda e, d=dhi, p=par: e.dma_start(d, ph_b[:, p, 0, :]),
                     inc=("PRE", 16))
                wr = b.op("scalar", lambda e, d=dlo, p=par: e.dma_start(d, ph_b[:, p, 1, :]),
                          inc=("PRE", 16))
                pp["write_cap"][i] = wr
                scr_done[key] = max(scr_done.get(key, 0), wr)
                pp["split_i"] = i + 1

        emit_prepass(18)

        x_loaded = [None] * NTILE
        x_loaded[0] = b.op("sync", lambda e: e.dma_start(
            xm[:, :, :], xT.rearrange("(kt p) (jj t) -> p kt jj t", p=128, t=TT)[:, :, 0, :]),
            inc=("LD", 16))

        wq_stream = {}
        wq_last_use = {}
        wf_last_use = {}
        split_done = [None] * KT
        x2_done_ref = [[None] * KT]

        def ln_phase(j, which):
            x2d = x2_done_ref[0]
            b.guard_bank(4)
            b.guard_bank(3)
            if which == "ln1":
                b.wait("tensor", "LD", x_loaded[j])
            sumx_done = None
            for kt in range(KT):
                if which == "ln2":
                    b.wait("tensor", "VE", x2d[kt])
                sumx_done = b.op("tensor", lambda e, kt=kt: e.matmul(
                    ps[0:1, 4, :], ones_c[:, :], xm[:, kt, :],
                    start=(kt == 0), stop=(kt == KT - 1)),
                    inc=("TE", 1) if kt == KT - 1 else None)
            sq_read = [None, None]
            sumsq_done = None
            if which == "ln1":
                b.wait("scalar", "LD", x_loaded[j])
            for kt in range(KT):
                par = kt % 2
                if sq_read[par] is not None:
                    b.wait("scalar", "TE", sq_read[par])
                if which == "ln2":
                    b.wait("scalar", "VE", x2d[kt])
                se = b.op("scalar", lambda e, kt=kt, p=par: e.activation(
                    sq[:, p, :], xm[:, kt, :], AF.Square), inc=("SE", 1))
                b.wait("tensor", "SE", se)
                sumsq_done = b.op("tensor", lambda e, p=par, kt=kt: e.matmul(
                    ps[0:1, 3, :], ones_c[:, :], sq[:, p, :],
                    start=(kt == 0), stop=(kt == KT - 1)), inc=("TE", 1))
                sq_read[par] = sumsq_done
            b.wait("vector", "TE", sumsq_done)
            b.op("vector", lambda e: e.tensor_scalar_mul(stats[0:1, 0, :], ps[0:1, 4, :], INV_DM))
            b.op("vector", lambda e: e.tensor_scalar_mul(stats[0:1, 1, :], ps[0:1, 3, :], INV_DM))
            ve_b4 = b.op("vector", lambda e: e.tensor_mul(
                stats[0:1, 2, :], stats[0:1, 0, :], stats[0:1, 0, :]), inc=("VE", 1))
            b.set_guard(4, "VE", ve_b4)
            b.set_guard(3, "VE", ve_b4)
            b.op("vector", lambda e: e.tensor_sub(stats[0:1, 2, :], stats[0:1, 1, :],
                                                  stats[0:1, 2, :]))
            ve_var = b.op("vector", lambda e: e.tensor_scalar_add(
                stats[0:1, 2, :], stats[0:1, 2, :], EPS_LN), inc=("VE", 1))
            b.wait("scalar", "VE", ve_var)
            se_sqrt = b.op("scalar", lambda e: e.activation(
                stats[0:1, 3, :], stats[0:1, 2, :], AF.Sqrt), inc=("SE", 1))
            b.wait("vector", "SE", se_sqrt)
            ve_rstd = b.op("vector", lambda e: e.reciprocal(stats[0:1, 3, :], stats[0:1, 3, :]),
                           inc=("VE", 1))
            b.guard_bank(5)
            b.guard_bank(6)
            b.wait("tensor", "VE", ve_b4)
            te_mb = b.op("tensor", lambda e: e.matmul(ps[:, 5, :], ones_r[:, :],
                                                      stats[0:1, 0, :], start=True, stop=True),
                         inc=("TE", 1))
            b.wait("tensor", "VE", ve_rstd)
            te_rb = b.op("tensor", lambda e: e.matmul(ps[:, 6, :], ones_r[:, :],
                                                      stats[0:1, 3, :], start=True, stop=True),
                         inc=("TE", 1))
            b.wait("vector", "TE", te_mb)
            b.op("vector", lambda e: e.tensor_copy(mb_s[:, :], ps[:, 5, :]))
            b.wait("vector", "TE", te_rb)
            ve_bc = b.op("vector", lambda e: e.tensor_copy(rb_s[:, :], ps[:, 6, :]), inc=("VE", 1))
            b.set_guard(5, "VE", ve_bc)
            b.set_guard(6, "VE", ve_bc)
            for kt in range(KT):
                par = kt % 2
                b.op("vector", lambda e, kt=kt, p=par: e.tensor_sub(
                    xnt[:, p, :], xm[:, kt, :], mb_s[:, :]))
                b.op("vector", lambda e, p=par: e.tensor_mul(
                    xnt[:, p, :], xnt[:, p, :], rb_s[:, :]))
                b.op("vector", lambda e, kt=kt, p=par: e.tensor_copy(
                    xnh[:, kt, :], xnt[:, p, :]))
                split_done[kt] = b.op("vector", lambda e, kt=kt, p=par: e.tensor_sub(
                    xnl[:, kt, :], xnt[:, p, :], xnh[:, kt, :]), inc=("VE", 1))
                emit_prepass(2)

        def mm3(bank, lhs_hi, lhs_lo, rhs_hi, rhs_lo, start, stop, inc_last=False,
                want_inc=False):
            b.op("tensor", lambda e: e.matmul(ps[:, bank, :], lhs_hi, rhs_hi,
                                              start=start, stop=False))
            b.op("tensor", lambda e: e.matmul(ps[:, bank, :], lhs_hi, rhs_lo,
                                              start=False, stop=False))
            return b.op("tensor", lambda e: e.matmul(ps[:, bank, :], lhs_lo, rhs_hi,
                                                     start=False, stop=stop),
                        inc=("TE", 1) if ((stop and inc_last) or want_inc) else None)

        def attention(j):
            qkv_te = {}
            w_done = {}
            den_te = {}
            recip_done = {}
            bc_te = {}
            attn_split = {}

            def stream_tblock(h, t3):
                slot = (h * 3 + t3) % 2
                kk = scr_done.get(("qkv", h, t3))
                assert kk is not None, f"scratch not ready for qkv head {h} t {t3}"
                b.wait("sync", "PRE", kk)
                lu = wq_last_use.get(slot)
                if lu is not None:
                    b.wait("sync", "TE", lu)
                b.op("sync", lambda e, h=h, t3=t3, s=slot: e.dma_start(
                    wq_b[:, s, 0, :, :].rearrange("p kt c -> p (kt c)"),
                    s_qkv[h, t3, 0, :, :]), inc=("LD", 16))
                v = b.op("sync", lambda e, h=h, t3=t3, s=slot: e.dma_start(
                    wq_b[:, s, 1, :, :].rearrange("p kt c -> p (kt c)"),
                    s_qkv[h, t3, 1, :, :]), inc=("LD", 16))
                wq_stream[(h, t3)] = (v, slot)

            def pe_ttile(h, t3, bank):
                v, slot = wq_stream[(h, t3)]
                b.wait("tensor", "LD", v)
                b.guard_bank(bank)
                last = None
                for kt in range(KT):
                    b.wait("tensor", "VE", split_done[kt])
                    last = mm3(bank, wq_b[:, slot, 0, kt, :], wq_b[:, slot, 1, kt, :],
                               xnh[:, kt, :], xnl[:, kt, :],
                               start=(kt == 0), stop=(kt == KT - 1), inc_last=True)
                wq_last_use[slot] = last
                return last

            stream_tblock(0, 0)
            stream_tblock(0, 1)

            for it in range(H + 2):
                h = it
                emit_prepass(6)
                if h < H:
                    p3 = 3 * (h % 2)
                    teq = pe_ttile(h, 0, p3 + 0)
                    stream_tblock(h, 2)
                    tek = pe_ttile(h, 1, p3 + 1)
                if 0 <= h - 1 < H:
                    hp = h - 1
                    b.wait("tensor", "VE", w_done[hp])
                    b.guard_bank(6)
                    den_te[hp] = b.op("tensor", lambda e: e.matmul(
                        ps[0:1, 6, :], ones_c[:, :], w_s[:, :], start=True, stop=True),
                        inc=("TE", 1))
                if h < H:
                    tev = pe_ttile(h, 2, p3 + 2)
                    if h + 1 < H:
                        stream_tblock(h + 1, 0)
                        stream_tblock(h + 1, 1)
                    qkv_te[h] = (teq, tek, tev)
                if 0 <= h - 2 < H:
                    hp = h - 2
                    b.wait("tensor", "VE", recip_done[hp])
                    b.guard_bank(7)
                    bc_te[hp] = b.op("tensor", lambda e, hp=hp: e.matmul(
                        ps[:, 7, :], ones_r[:, :], rden_s[0:1, hp % 2, :],
                        start=True, stop=True), inc=("TE", 1))

                if 0 <= h - 1 < H:
                    hp = h - 1
                    b.wait("vector", "TE", den_te[hp])
                    if hp - 2 >= 0:
                        b.wait("vector", "TE", bc_te[hp - 2])
                    b.op("vector", lambda e: e.tensor_scalar_max(
                        stats[0:1, 4, :], ps[0:1, 6, :], EPS_DEN))
                    cr = b.op("vector", lambda e, hp=hp: e.reciprocal(
                        rden_s[0:1, hp % 2, :], stats[0:1, 4, :]), inc=("VE", 1))
                    b.set_guard(6, "VE", cr)
                    recip_done[hp] = cr

                if 0 <= h - 2 < H:
                    hp = h - 2
                    b.wait("vector", "TE", bc_te[hp])
                    b.op("vector", lambda e, hp=hp: e.tensor_mul(
                        of_s[:, :], u_s[:, hp % 2, :], ps[:, 7, :]))
                    ord_ = b.op("vector", lambda e, hp=hp: e.tensor_copy(
                        ath[:, hp, :], of_s[:, :]), inc=("VE", 1))
                    b.set_guard(7, "VE", ord_)
                    attn_split[hp] = b.op("vector", lambda e, hp=hp: e.tensor_sub(
                        atl[:, hp, :], of_s[:, :], ath[:, hp, :]), inc=("VE", 1))

                if h < H:
                    teq, tek, tev = qkv_te[h]
                    p3 = 3 * (h % 2)
                    b.wait("vector", "TE", teq)
                    ve_minq = b.op("vector", lambda e, p3=p3: e.tensor_scalar_min(
                        sq[:, 1, :], ps[:, p3 + 0, :], 0.0), inc=("VE", 1))
                    b.wait("scalar", "VE", ve_minq)
                    se_eq = b.op("scalar", lambda e: e.activation(
                        ex_s[:, :], sq[:, 1, :], AF.Exp), inc=("SE", 1))
                    b.wait("vector", "SE", se_eq)
                    ve_q = b.op("vector", lambda e, p3=p3: e.scalar_tensor_tensor(
                        q_s[:, :], ps[:, p3 + 0, :], 0.0, ex_s[:, :], ALU.max, ALU.add),
                        inc=("VE", 1))
                    b.set_guard(p3 + 0, "VE", ve_q)
                    b.wait("vector", "TE", tek)
                    ve_mink = b.op("vector", lambda e, p3=p3: e.tensor_scalar_min(
                        sq[:, 1, :], ps[:, p3 + 1, :], 0.0), inc=("VE", 1))
                    b.wait("scalar", "VE", ve_mink)
                    se_ek = b.op("scalar", lambda e: e.activation(
                        ex_s[:, :], sq[:, 1, :], AF.Exp), inc=("SE", 1))
                    b.wait("vector", "SE", se_ek)
                    ve_k = b.op("vector", lambda e, p3=p3: e.scalar_tensor_tensor(
                        k_s[:, :], ps[:, p3 + 1, :], 0.0, ex_s[:, :], ALU.max, ALU.add),
                        inc=("VE", 1))
                    b.set_guard(p3 + 1, "VE", ve_k)
                    b.wait("vector", "TE", tev)
                    ve_kv = b.op("vector", lambda e, p3=p3: e.tensor_mul(
                        kv_s[:, :], k_s[:, :], ps[:, p3 + 2, :]), inc=("VE", 1))
                    b.set_guard(p3 + 2, "VE", ve_kv)
                    b.op("vector", lambda e, h=h: e.tensor_tensor_scan(
                        rkv_s[:, :], dec_s[:, h:h + 1].broadcast_to([128, TT]), kv_s[:, :],
                        ckv_s[:, h:h + 1], ALU.mult, ALU.add))
                    b.op("vector", lambda e, h=h: e.tensor_tensor_scan(
                        rks_s[:, :], dec_s[:, h:h + 1].broadcast_to([128, TT]), k_s[:, :],
                        cks_s[:, h:h + 1], ALU.mult, ALU.add))
                    b.op("vector", lambda e, h=h: e.tensor_mul(
                        u_s[:, h % 2, :], q_s[:, :], rkv_s[:, :]))
                    b.op("vector", lambda e, h=h: e.tensor_copy(
                        ckv_s[:, h:h + 1], rkv_s[:, TT - 1:TT]))
                    b.op("vector", lambda e, h=h: e.tensor_copy(
                        cks_s[:, h:h + 1], rks_s[:, TT - 1:TT]))
                    w_done[h] = b.op("vector", lambda e: e.tensor_mul(
                        w_s[:, :], q_s[:, :], rks_s[:, :]), inc=("VE", 1))
                    if debug and j == dbg_tile and h == 0:
                        b.wait("gpsimd", "VE", w_done[h])
                        for nm, src in (("q0", q_s), ("k0", k_s), ("kv0", kv_s),
                                        ("rkv0", rkv_s), ("rks0", rks_s)):
                            cap = b.op("gpsimd", lambda e, nm=nm, src=src: e.dma_start(
                                dbg[nm], src[:, :]), inc=("YS", 16))
                        b.wait("vector", "YS", cap)

            return attn_split[H - 1]

        def out_proj(j, attn_last):
            x2d = [None] * KT
            for m in range(16):
                slot = m % 2
                kk = scr_done.get(("out", m))
                assert kk is not None
                b.wait("sync", "PRE", kk)
                lu = wq_last_use.get(slot)
                if lu is not None:
                    b.wait("sync", "TE", lu)
                b.op("sync", lambda e, m=m, s=slot: e.dma_start(
                    wo_b[:, s, 0, :, :].rearrange("p kt c -> p (kt c)"), s_out[m, 0, :, :]),
                    inc=("LD", 16))
                v = b.op("sync", lambda e, m=m, s=slot: e.dma_start(
                    wo_b[:, s, 1, :, :].rearrange("p kt c -> p (kt c)"), s_out[m, 1, :, :]),
                    inc=("LD", 16))
                bank = m % 4
                b.wait("tensor", "LD", v)
                if m == 0:
                    b.wait("tensor", "VE", attn_last)
                b.guard_bank(bank)
                last = None
                for kt in range(KT):
                    last = mm3(bank, wo_b[:, slot, 0, kt, :], wo_b[:, slot, 1, kt, :],
                               ath[:, kt, :], atl[:, kt, :],
                               start=(kt == 0), stop=(kt == KT - 1), inc_last=True)
                wq_last_use[slot] = last
                b.wait("vector", "TE", last)
                ve = b.op("vector", lambda e, m=m, bank=bank: e.tensor_add(
                    xm[:, m, :], xm[:, m, :], ps[:, bank, :]), inc=("VE", 1))
                b.set_guard(bank, "VE", ve)
                x2d[m] = ve
                emit_prepass(2)
            if debug and j == dbg_tile:
                b.wait("gpsimd", "VE", x2d[15])
                c1 = b.op("gpsimd", lambda e: e.dma_start(dbg["x2"], xm[:, :, :]), inc=("YS", 16))
                c2 = b.op("gpsimd", lambda e: e.dma_start(dbg["att_hi"], ath[:, :, :]),
                          inc=("YS", 16))
                b.wait("vector", "YS", c2)
            return x2d

        def ffn(j):
            last_acc = None
            prev_h_done = [None] * 8
            for hc in range(8):
                emit_prepass(48)
                gate_done = [None] * 8
                for kd in range(KT):
                    slot = kd % 2
                    kk = scr_done.get(("gate", hc, kd))
                    assert kk is not None
                    b.wait("sync", "PRE", kk)
                    lu = wf_last_use.get(slot)
                    if lu is not None:
                        b.wait("sync", "TE", lu)
                    b.op("sync", lambda e, hc=hc, kd=kd, s=slot: e.dma_start(
                        wf_b[:, s, 0, :], s_gate[hc, kd, 0, :, :]), inc=("LD", 16))
                    v = b.op("sync", lambda e, hc=hc, kd=kd, s=slot: e.dma_start(
                        wf_b[:, s, 1, :], s_gate[hc, kd, 1, :, :]), inc=("LD", 16))
                    b.wait("tensor", "LD", v)
                    b.wait("tensor", "VE", split_done[kd])
                    last = None
                    for m in range(8):
                        if kd == 0:
                            b.guard_bank(m)
                        last = mm3(m, wf_b[:, slot, 0, m * 128:(m + 1) * 128],
                                   wf_b[:, slot, 1, m * 128:(m + 1) * 128],
                                   xnh[:, kd, :], xnl[:, kd, :],
                                   start=(kd == 0), stop=(kd == KT - 1), inc_last=True,
                                   want_inc=(m == 7))
                        if kd == KT - 1:
                            gate_done[m] = last
                    wf_last_use[slot] = last
                silu_done = [None] * 8
                for m in range(8):
                    b.wait("scalar", "TE", gate_done[m])
                    if prev_h_done[m] is not None:
                        b.wait("scalar", "VE", prev_h_done[m])
                    silu_done[m] = b.op("scalar", lambda e, m=m: e.activation(
                        gate_s[:, m, :], ps[:, m, :], AF.Silu), inc=("SE", 1))
                up_done = [None] * 8
                for kd in range(KT):
                    slot = kd % 2
                    kk = scr_done.get(("up", hc, kd))
                    b.wait("sync", "PRE", kk)
                    lu = wf_last_use.get(slot)
                    if lu is not None:
                        b.wait("sync", "TE", lu)
                    b.op("sync", lambda e, hc=hc, kd=kd, s=slot: e.dma_start(
                        wf_b[:, s, 0, :], (s_gate if UP_USES_GATE else s_up)[hc, kd, 0, :, :]), inc=("LD", 16))
                    v = b.op("sync", lambda e, hc=hc, kd=kd, s=slot: e.dma_start(
                        wf_b[:, s, 1, :], (s_gate if UP_USES_GATE else s_up)[hc, kd, 1, :, :]), inc=("LD", 16))
                    b.wait("tensor", "LD", v)
                    last = None
                    for m in range(8):
                        if kd == 0:
                            b.wait("tensor", "SE", silu_done[m])
                        last = mm3(m, wf_b[:, slot, 0, m * 128:(m + 1) * 128],
                                   wf_b[:, slot, 1, m * 128:(m + 1) * 128],
                                   xnh[:, kd, :], xnl[:, kd, :],
                                   start=(kd == 0), stop=(kd == KT - 1), inc_last=True,
                                   want_inc=(m == 7))
                        if kd == KT - 1:
                            up_done[m] = last
                    wf_last_use[slot] = last
                h_done = [None] * 8
                for m in range(8):
                    b.wait("vector", "TE", up_done[m])
                    if debug and j == dbg_tile and hc == 0:
                        b.op("vector", lambda e, m=m: e.tensor_copy(
                            updbg[:, m, :], ps[:, m, 0:TT // 2]))
                    b.op("vector", lambda e, m=m: e.tensor_mul(
                        hf_s[:, :], gate_s[:, m, :], ps[:, m, :]))
                    b.op("vector", lambda e, m=m: e.tensor_copy(ath[:, m, :], hf_s[:, :]))
                    ve = b.op("vector", lambda e, m=m: e.tensor_sub(
                        atl[:, m, :], hf_s[:, :], ath[:, m, :]), inc=("VE", 1))
                    b.set_guard(m, "VE", ve)
                    h_done[m] = ve
                    prev_h_done[m] = ve
                if debug and j == dbg_tile and hc == 0:
                    b.wait("gpsimd", "VE", h_done[7])
                    b.op("gpsimd", lambda e: e.dma_start(dbg["h0"], ath[:, 0:8, :]),
                         inc=("YS", 16))
                    b.op("gpsimd", lambda e: e.dma_start(dbg["h0lo"], atl[:, 0:8, :]),
                         inc=("YS", 16))
                    b.op("gpsimd", lambda e: e.dma_start(dbg["up0"], updbg[:, :, :]),
                         inc=("YS", 16))
                    ch = b.op("gpsimd", lambda e: e.dma_start(dbg["gate0"], gate_s[:, :, :]),
                              inc=("YS", 16))
                    b.wait("vector", "YS", ch)
                for dh in range(2):
                    down_done = [None] * 8
                    for kh in range(8):
                        slot = kh % 2
                        kk = scr_done.get(("down", hc, dh, kh))
                        b.wait("sync", "PRE", kk)
                        lu = wf_last_use.get(slot)
                        if lu is not None:
                            b.wait("sync", "TE", lu)
                        b.op("sync", lambda e, hc=hc, dh=dh, kh=kh, s=slot: e.dma_start(
                            wf_b[:, s, 0, :], s_down[hc, dh, kh, 0, :, :]), inc=("LD", 16))
                        v = b.op("sync", lambda e, hc=hc, dh=dh, kh=kh, s=slot: e.dma_start(
                            wf_b[:, s, 1, :], s_down[hc, dh, kh, 1, :, :]), inc=("LD", 16))
                        b.wait("tensor", "LD", v)
                        b.wait("tensor", "VE", h_done[kh])
                        last = None
                        for m in range(8):
                            if kh == 0:
                                b.guard_bank(m)
                            last = mm3(m, wf_b[:, slot, 0, m * 128:(m + 1) * 128],
                                       wf_b[:, slot, 1, m * 128:(m + 1) * 128],
                                       ath[:, kh, :], atl[:, kh, :],
                                       start=(kh == 0), stop=(kh == 7), inc_last=True,
                                       want_inc=(m == 7))
                            if kh == 7:
                                down_done[m] = last
                        wf_last_use[slot] = last
                    for m in range(8):
                        dt_ = dh * 8 + m
                        b.wait("vector", "TE", down_done[m])
                        ve = b.op("vector", lambda e, m=m, dt_=dt_: e.tensor_add(
                            xm[:, dt_, :], xm[:, dt_, :], ps[:, m, :]), inc=("VE", 1))
                        b.set_guard(m, "VE", ve)
                        last_acc = ve
            return last_acc

        for j in range(NTILE):
            ln_phase(j, "ln1")
            attn_last = attention(j)
            if debug and j == dbg_tile:
                b.wait("gpsimd", "VE", split_done[15])
                cx = b.op("gpsimd", lambda e: e.dma_start(dbg["xn_hi"], xnh[:, :, :]),
                          inc=("YS", 16))
                b.wait("vector", "YS", cx)
            x2_done_ref[0] = out_proj(j, attn_last)
            ln_phase(j, "ln2")
            if debug and j == dbg_tile:
                b.wait("gpsimd", "VE", split_done[15])
                cx2 = b.op("gpsimd", lambda e: e.dma_start(dbg["xn2_hi"], xnh[:, :, :]),
                           inc=("YS", 16))
                b.wait("vector", "YS", cx2)
            last_acc = ffn(j)
            b.wait("gpsimd", "VE", last_acc)
            ys = b.op("gpsimd", lambda e, j=j: e.dma_start(
                y.rearrange("(kt p) (jj t) -> p kt jj t", p=128, t=TT)[:, :, j, :],
                xm[:, :, :]), inc=("YS", 16))
            if j + 1 < NTILE:
                b.wait("sync", "YS", ys)
                x_loaded[j + 1] = b.op("sync", lambda e, j=j: e.dma_start(
                    xm[:, :, :],
                    xT.rearrange("(kt p) (jj t) -> p kt jj t", p=128, t=TT)[:, :, j + 1, :]),
                    inc=("LD", 16))

        emit_prepass(len(chunks))
        assert pp["split_i"] == len(chunks), f"prepass incomplete {pp['split_i']}/{len(chunks)}"
        b.wait("sync", "YS", b.cnt["YS"])

        with nc.Block() as block:
            @block.sync
            def _(sync):
                for f in b.ops["sync"]:
                    f(sync)

            @block.scalar
            def _(scalar):
                for f in b.ops["scalar"]:
                    f(scalar)

            @block.vector
            def _(vector):
                for f in b.ops["vector"]:
                    f(vector)

            @block.tensor
            def _(tensor):
                for f in b.ops["tensor"]:
                    f(tensor)

            @block.gpsimd
            def _(gpsimd):
                for f in b.ops["gpsimd"]:
                    f(gpsimd)

    return nc


def stage_weights(w_qkv, w_out, w_gate, w_up, w_down, decay):
    wq = np.asarray(w_qkv, dtype=np.float32)
    wo = np.asarray(w_out, dtype=np.float32)
    wg = np.asarray(w_gate, dtype=np.float32)
    wu = np.asarray(w_up, dtype=np.float32)
    wd = np.asarray(w_down, dtype=np.float32)
    qkvW = np.ascontiguousarray(
        wq.reshape(KT, 128, 3, H, 128).transpose(3, 2, 1, 0, 4).reshape(H, 3, 128, KT * 128))
    outW = np.ascontiguousarray(
        wo.reshape(KT, 128, 16, 128).transpose(2, 1, 0, 3).reshape(16, 128, KT * 128))
    gateW = np.ascontiguousarray(wg.reshape(8, 1024, KT, 128).transpose(0, 2, 3, 1))
    upW = np.ascontiguousarray(wu.reshape(8, 1024, KT, 128).transpose(0, 2, 3, 1))
    downW = np.ascontiguousarray(wd.reshape(2, 1024, 8, 8, 128).transpose(2, 0, 3, 4, 1))
    dkv = np.ascontiguousarray(np.asarray(decay, dtype=np.float32).reshape(1, H))
    return qkvW, outW, gateW, upW, downW, dkv


def kernel(L0_attn_x, L0_attn_w_qkv, L0_attn_w_out, L0_attn_decay_param,
           L0_ffn_w_gate, L0_ffn_w_up, L0_ffn_w_down):
    x = np.asarray(L0_attn_x, dtype=np.float32)
    qkvW, outW, gateW, upW, downW, dkv = stage_weights(
        L0_attn_w_qkv, L0_attn_w_out, L0_ffn_w_gate, L0_ffn_w_up,
        L0_ffn_w_down, L0_attn_decay_param)
    nc = build_program(debug=False)
    in_maps = []
    for b_ in range(NCORES):
        in_maps.append({
            "xT": np.ascontiguousarray(x[b_].T),
            "qkvW": qkvW, "outW": outW, "gateW": gateW, "upW": upW,
            "downW": downW, "dk": dkv,
        })
    res = run_bass_kernel_spmd(nc, in_maps, core_ids=list(range(NCORES)))
    out = np.empty((NCORES, T, DM), dtype=np.float32)
    for b_ in range(NCORES):
        out[b_] = res.results[b_]["y"].T
    return out


# version 02
# speedup vs baseline: 1.0340x; 1.0340x over previous
import contextlib
import numpy as np
import concourse.bass as bass
import concourse.mybir as mybir
from concourse.bass_utils import run_bass_kernel_spmd

F32 = mybir.dt.float32
BF16 = mybir.dt.bfloat16
AF = mybir.ActivationFunctionType
ALU = mybir.AluOpType

NCORES = 8
T = 2048
DM = 2048
H = 16
KT = DM // 128
FF = 8192
TT = 512
NTILE = T // TT
EPS_LN = 1e-5
EPS_DEN = 1e-6
INV_DM = 1.0 / DM
DBG_TILE = 0
UP_USES_GATE = False
SAFE_SLAB = False


class Builder:
    POOLS = {"sync": ["LDa", "LDb", "LDc", "LDd", "LDe", "LDf", "LDg", "LDh"],
             "scalar": ["PRa", "PRb", "PRc", "PRd", "PRe", "PRf", "PRg", "PRh"],
             "gpsimd": ["YSa", "YSb", "YSc", "YSd"]}

    def __init__(self, nc):
        self.nc = nc
        self.ops = {"sync": [], "scalar": [], "vector": [], "tensor": [], "gpsimd": []}
        self.cnt = {}
        self.sems = {}
        self.bank_guard = {}
        self.qidx = {k: 0 for k in self.POOLS}

    def dma(self, eng, build):
        pool = self.POOLS[eng]
        semname = pool[self.qidx[eng] % len(pool)]
        self.qidx[eng] += 1
        val = self.inc_of(semname, 16)
        h = self.sems[semname]
        self.emit(eng, lambda e, b=build, h=h: b(e).then_inc(h, 16))
        return (semname, val)

    def wait_tok(self, eng, tok):
        if tok is None:
            return
        if isinstance(tok, list):
            for t in tok:
                self.wait_tok(eng, t)
            return
        self.wait(eng, tok[0], tok[1])

    def emit(self, eng, fn):
        self.ops[eng].append(fn)

    def wait(self, eng, semname, val):
        if val is None or val <= 0:
            return
        h = self.sems[semname]
        self.emit(eng, lambda e, h=h, v=val: e.wait_ge(h, v))

    def inc_of(self, semname, n=1):
        self.cnt[semname] = self.cnt.get(semname, 0) + n
        return self.cnt[semname]

    def op(self, eng, build, inc=None):
        if inc is not None:
            semname, n = inc
            val = self.inc_of(semname, n)
            h = self.sems[semname]
            self.emit(eng, lambda e, b=build, h=h, n=n: b(e).then_inc(h, n))
            return val
        self.emit(eng, lambda e, b=build: b(e))
        return None

    def guard_bank(self, bank, eng="tensor"):
        g = self.bank_guard.get(bank)
        if g is not None:
            self.wait(eng, g[0], g[1])

    def set_guard(self, bank, semname, val):
        self.bank_guard[bank] = (semname, val)


def build_program(debug=False, dbg_tile=0):
    nc = bass.Bass("TRN2", target_bir_lowering=False)

    xT = nc.dram_tensor("xT", [DM, T], F32, kind="ExternalInput").ap()
    qkvW = nc.dram_tensor("qkvW", [H, 3, 128, KT * 128], F32, kind="ExternalInput").ap()
    outW = nc.dram_tensor("outW", [16, 128, KT * 128], F32, kind="ExternalInput").ap()
    gateW = nc.dram_tensor("gateW", [8, KT, 128, 1024], F32, kind="ExternalInput").ap()
    upW = nc.dram_tensor("upW", [8, KT, 128, 1024], F32, kind="ExternalInput").ap()
    downW = nc.dram_tensor("downW", [8, 2, 8, 128, 1024], F32, kind="ExternalInput").ap()
    dk = nc.dram_tensor("dk", [1, H], F32, kind="ExternalInput").ap()
    y = nc.dram_tensor("y", [DM, T], F32, kind="ExternalOutput").ap()

    s_qkv = nc.dram_tensor("s_qkv", [H, 3, 2, 128, KT * 128], BF16).ap()
    s_out = nc.dram_tensor("s_out", [16, 2, 128, KT * 128], BF16).ap()
    s_gate = nc.dram_tensor("s_gate", [8, KT, 2, 128, 1024], BF16).ap()
    s_up = nc.dram_tensor("s_up", [8, KT, 2, 128, 1024], BF16).ap()
    s_down = nc.dram_tensor("s_down", [8, 2, 8, 2, 128, 1024], BF16).ap()

    dbg = {}
    if debug:
        dbg["xn_hi"] = nc.dram_tensor("d_xnhi", [128, KT, TT], BF16, kind="ExternalOutput").ap()
        dbg["q0"] = nc.dram_tensor("d_q0", [128, TT], F32, kind="ExternalOutput").ap()
        dbg["k0"] = nc.dram_tensor("d_k0", [128, TT], F32, kind="ExternalOutput").ap()
        dbg["kv0"] = nc.dram_tensor("d_kv0", [128, TT], F32, kind="ExternalOutput").ap()
        dbg["rkv0"] = nc.dram_tensor("d_rkv0", [128, TT], F32, kind="ExternalOutput").ap()
        dbg["rks0"] = nc.dram_tensor("d_rks0", [128, TT], F32, kind="ExternalOutput").ap()
        dbg["att_hi"] = nc.dram_tensor("d_atthi", [128, KT, TT], BF16, kind="ExternalOutput").ap()
        dbg["x2"] = nc.dram_tensor("d_x2", [128, KT, TT], F32, kind="ExternalOutput").ap()
        dbg["h0"] = nc.dram_tensor("d_h0", [8, 128, 8, TT], BF16, kind="ExternalOutput").ap()
        dbg["gate0"] = nc.dram_tensor("d_gate0", [8, 128, 8, TT], F32, kind="ExternalOutput").ap()
        dbg["h0lo"] = nc.dram_tensor("d_h0lo", [8, 128, 8, TT], BF16, kind="ExternalOutput").ap()
        dbg["up0"] = nc.dram_tensor("d_up0", [128, 8, TT // 2], BF16, kind="ExternalOutput").ap()
        dbg["xn2_hi"] = nc.dram_tensor("d_xn2hi", [128, KT, TT], BF16, kind="ExternalOutput").ap()
        dbg["sgate"] = nc.dram_tensor("d_sgate", [8, KT, 2, 128, 1024], BF16, kind="ExternalOutput").ap()

    stack = contextlib.ExitStack()
    with stack:
        xm = stack.enter_context(nc.sbuf_tensor("xm", [128, KT, TT], F32))
        xnt = stack.enter_context(nc.sbuf_tensor("xnt", [128, 2, TT], F32))
        xnh = stack.enter_context(nc.sbuf_tensor("xnh", [128, KT, TT], BF16))
        xnl = stack.enter_context(nc.sbuf_tensor("xnl", [128, KT, TT], BF16))
        ath = stack.enter_context(nc.sbuf_tensor("ath", [128, KT, TT], BF16))
        atl = stack.enter_context(nc.sbuf_tensor("atl", [128, KT, TT], BF16))
        sq = stack.enter_context(nc.sbuf_tensor("sq", [128, 2, TT], F32))
        q_s = stack.enter_context(nc.sbuf_tensor("q_s", [128, TT], F32))
        k_s = stack.enter_context(nc.sbuf_tensor("k_s", [128, TT], F32))
        kv_s = stack.enter_context(nc.sbuf_tensor("kv_s", [128, TT], F32))
        rkv_s = stack.enter_context(nc.sbuf_tensor("rkv_s", [128, TT], F32))
        rks_s = stack.enter_context(nc.sbuf_tensor("rks_s", [128, TT], F32))
        u_s = stack.enter_context(nc.sbuf_tensor("u_s", [128, 2, TT], F32))
        w_s = stack.enter_context(nc.sbuf_tensor("w_s", [128, TT], F32))
        ex_s = stack.enter_context(nc.sbuf_tensor("ex_s", [128, TT], F32))
        of_s = stack.enter_context(nc.sbuf_tensor("of_s", [128, TT], F32))
        gate_s = stack.enter_context(nc.sbuf_tensor("gate_s", [128, 8, TT], F32))
        hf_s = stack.enter_context(nc.sbuf_tensor("hf_s", [128, TT], F32))
        stats = stack.enter_context(nc.sbuf_tensor("stats", [1, 5, TT], F32))
        rden_s = stack.enter_context(nc.sbuf_tensor("rden_s", [1, 2, TT], F32))
        mb_s = stack.enter_context(nc.sbuf_tensor("mb_s", [128, TT], F32))
        rb_s = stack.enter_context(nc.sbuf_tensor("rb_s", [128, TT], F32))
        wq_b = stack.enter_context(nc.sbuf_tensor("wq_b", [128, 2, 2, KT, 128], BF16))
        wo_b = wq_b
        wf_b = stack.enter_context(nc.sbuf_tensor("wf_b", [128, 2, 2, 1024], BF16))
        pf_b = stack.enter_context(nc.sbuf_tensor("pf_b", [128, 2, 1024], F32))
        ph_b = stack.enter_context(nc.sbuf_tensor("ph_b", [128, 2, 2, 1024], BF16))
        ones_c = stack.enter_context(nc.sbuf_tensor("ones_c", [128, 1], F32))
        ones_r = stack.enter_context(nc.sbuf_tensor("ones_r", [1, 128], F32))
        sig_s = stack.enter_context(nc.sbuf_tensor("sig_s", [1, H], F32))
        dec_s = stack.enter_context(nc.sbuf_tensor("dec_s", [128, H], F32))
        ckv_s = stack.enter_context(nc.sbuf_tensor("ckv_s", [128, H], F32))
        updbg = stack.enter_context(nc.sbuf_tensor("updbg", [128, 8, TT // 2], BF16)) if debug else None
        cks_s = stack.enter_context(nc.sbuf_tensor("cks_s", [128, H], F32))

        ps = stack.enter_context(nc.psum_tensor("ps", [128, 8, 512], F32))

        b = Builder(nc)
        all_sems = ["TE", "VE", "SE", "GE"]
        for pool in Builder.POOLS.values():
            all_sems.extend(pool)
        for name in all_sems:
            b.sems[name] = stack.enter_context(nc.semaphore(name))
            b.cnt[name] = 0

        ge_init = None
        for t_, v_ in ((ones_c, 1.0), (ones_r, 1.0), (ckv_s, 0.0), (cks_s, 0.0)):
            ge_init = b.op("gpsimd", lambda e, t=t_, v=v_: e.memset(t[:, :], v), inc=("GE", 1))

        ld_dk = b.dma("sync", lambda e: e.dma_start(sig_s[:, :], dk))
        b.wait_tok("scalar", ld_dk)
        se_sig = b.op("scalar", lambda e: e.activation(sig_s[0:1, :], sig_s[0:1, :], AF.Sigmoid),
                      inc=("SE", 1))
        b.wait("tensor", "GE", ge_init)
        b.wait("tensor", "SE", se_sig)
        te_dec = b.op("tensor", lambda e: e.matmul(ps[:, 5, 0:H], ones_r[:, :], sig_s[0:1, :],
                                                   start=True, stop=True), inc=("TE", 1))
        b.wait("vector", "TE", te_dec)
        b.wait("vector", "GE", ge_init)
        ve_dec = b.op("vector", lambda e: e.tensor_copy(dec_s[:, :], ps[:, 5, 0:H]), inc=("VE", 1))
        b.set_guard(5, "VE", ve_dec)

        chunks = []
        for h in range(H):
            for t3 in range(3):
                for c in range(2):
                    chunks.append((("qkv", h, t3),
                                   qkvW[h, t3, :, c * 1024:(c + 1) * 1024],
                                   s_qkv[h, t3, 0, :, c * 1024:(c + 1) * 1024],
                                   s_qkv[h, t3, 1, :, c * 1024:(c + 1) * 1024]))
        for m in range(16):
            for c in range(2):
                chunks.append((("out", m),
                               outW[m, :, c * 1024:(c + 1) * 1024],
                               s_out[m, 0, :, c * 1024:(c + 1) * 1024],
                               s_out[m, 1, :, c * 1024:(c + 1) * 1024]))
        for hc in range(8):
            for kd in range(KT):
                chunks.append((("gate", hc, kd), gateW[hc, kd, :, :],
                               s_gate[hc, kd, 0, :, :], s_gate[hc, kd, 1, :, :]))
            for kd in range(KT):
                chunks.append((("up", hc, kd), upW[hc, kd, :, :],
                               s_up[hc, kd, 0, :, :], s_up[hc, kd, 1, :, :]))
            for dh in range(2):
                for kh in range(8):
                    chunks.append((("down", hc, dh, kh), downW[hc, dh, kh, :, :],
                                   s_down[hc, dh, kh, 0, :, :], s_down[hc, dh, kh, 1, :, :]))

        scr_done = {}
        pp = {"load_i": 0, "split_i": 0, "split_cap": {}, "write_cap": {}, "load_cap": {}}

        def pp_ensure_loads(upto):
            while pp["load_i"] <= min(upto, len(chunks) - 1):
                i = pp["load_i"]
                par = i % 2
                prev_split = pp["split_cap"].get(i - 2)
                if prev_split is not None:
                    b.wait("scalar", "VE", prev_split)
                key, src, dhi, dlo = chunks[i]
                pp["load_cap"][i] = b.dma(
                    "scalar", lambda e, s=src, p=par: e.dma_start(pf_b[:, p, :], s))
                pp["load_i"] = i + 1

        def emit_prepass(n):
            for _ in range(n):
                i = pp["split_i"]
                if i >= len(chunks):
                    return
                pp_ensure_loads(i + 1)
                key, src, dhi, dlo = chunks[i]
                par = i % 2
                b.wait_tok("vector", pp["load_cap"][i])
                b.wait_tok("vector", pp["write_cap"].get(i - 2))
                b.op("vector", lambda e, p=par: e.tensor_copy(ph_b[:, p, 0, :], pf_b[:, p, :]))
                sp = b.op("vector", lambda e, p=par: e.tensor_sub(
                    ph_b[:, p, 1, :], pf_b[:, p, :], ph_b[:, p, 0, :]), inc=("VE", 1))
                pp["split_cap"][i] = sp
                b.wait("scalar", "VE", sp)
                w1 = b.dma("scalar", lambda e, d=dhi, p=par: e.dma_start(d, ph_b[:, p, 0, :]))
                w2 = b.dma("scalar", lambda e, d=dlo, p=par: e.dma_start(d, ph_b[:, p, 1, :]))
                pp["write_cap"][i] = [w1, w2]
                scr_done.setdefault(key, []).extend([w1, w2])
                pp["split_i"] = i + 1

        emit_prepass(18)

        x_loaded = [None] * NTILE
        x_loaded[0] = b.dma("sync", lambda e: e.dma_start(
            xm[:, :, :], xT.rearrange("(kt p) (jj t) -> p kt jj t", p=128, t=TT)[:, :, 0, :]))

        wq_stream = {}
        wq_last_use = {}
        wf_last_use = {}
        split_done = [None] * KT
        x2_done_ref = [[None] * KT]

        def ln_phase(j, which):
            x2d = x2_done_ref[0]
            b.guard_bank(4)
            b.guard_bank(3)
            if which == "ln1":
                b.wait_tok("tensor", x_loaded[j])
            sumx_done = None
            for kt in range(KT):
                if which == "ln2":
                    b.wait("tensor", "VE", x2d[kt])
                sumx_done = b.op("tensor", lambda e, kt=kt: e.matmul(
                    ps[0:1, 4, :], ones_c[:, :], xm[:, kt, :],
                    start=(kt == 0), stop=(kt == KT - 1)),
                    inc=("TE", 1) if kt == KT - 1 else None)
            sq_read = [None, None]
            sumsq_done = None
            if which == "ln1":
                b.wait_tok("scalar", x_loaded[j])
            for kt in range(KT):
                par = kt % 2
                if sq_read[par] is not None:
                    b.wait("scalar", "TE", sq_read[par])
                if which == "ln2":
                    b.wait("scalar", "VE", x2d[kt])
                se = b.op("scalar", lambda e, kt=kt, p=par: e.activation(
                    sq[:, p, :], xm[:, kt, :], AF.Square), inc=("SE", 1))
                b.wait("tensor", "SE", se)
                sumsq_done = b.op("tensor", lambda e, p=par, kt=kt: e.matmul(
                    ps[0:1, 3, :], ones_c[:, :], sq[:, p, :],
                    start=(kt == 0), stop=(kt == KT - 1)), inc=("TE", 1))
                sq_read[par] = sumsq_done
            b.wait("vector", "TE", sumsq_done)
            b.op("vector", lambda e: e.tensor_scalar_mul(stats[0:1, 0, :], ps[0:1, 4, :], INV_DM))
            b.op("vector", lambda e: e.tensor_scalar_mul(stats[0:1, 1, :], ps[0:1, 3, :], INV_DM))
            ve_b4 = b.op("vector", lambda e: e.tensor_mul(
                stats[0:1, 2, :], stats[0:1, 0, :], stats[0:1, 0, :]), inc=("VE", 1))
            b.set_guard(4, "VE", ve_b4)
            b.set_guard(3, "VE", ve_b4)
            b.op("vector", lambda e: e.tensor_sub(stats[0:1, 2, :], stats[0:1, 1, :],
                                                  stats[0:1, 2, :]))
            ve_var = b.op("vector", lambda e: e.tensor_scalar_add(
                stats[0:1, 2, :], stats[0:1, 2, :], EPS_LN), inc=("VE", 1))
            b.wait("scalar", "VE", ve_var)
            se_sqrt = b.op("scalar", lambda e: e.activation(
                stats[0:1, 3, :], stats[0:1, 2, :], AF.Sqrt), inc=("SE", 1))
            b.wait("vector", "SE", se_sqrt)
            ve_rstd = b.op("vector", lambda e: e.reciprocal(stats[0:1, 3, :], stats[0:1, 3, :]),
                           inc=("VE", 1))
            b.guard_bank(5)
            b.guard_bank(6)
            b.wait("tensor", "VE", ve_b4)
            te_mb = b.op("tensor", lambda e: e.matmul(ps[:, 5, :], ones_r[:, :],
                                                      stats[0:1, 0, :], start=True, stop=True),
                         inc=("TE", 1))
            b.wait("tensor", "VE", ve_rstd)
            te_rb = b.op("tensor", lambda e: e.matmul(ps[:, 6, :], ones_r[:, :],
                                                      stats[0:1, 3, :], start=True, stop=True),
                         inc=("TE", 1))
            b.wait("vector", "TE", te_mb)
            b.op("vector", lambda e: e.tensor_copy(mb_s[:, :], ps[:, 5, :]))
            b.wait("vector", "TE", te_rb)
            ve_bc = b.op("vector", lambda e: e.tensor_copy(rb_s[:, :], ps[:, 6, :]), inc=("VE", 1))
            b.set_guard(5, "VE", ve_bc)
            b.set_guard(6, "VE", ve_bc)
            for kt in range(KT):
                par = kt % 2
                b.op("vector", lambda e, kt=kt, p=par: e.tensor_sub(
                    xnt[:, p, :], xm[:, kt, :], mb_s[:, :]))
                b.op("vector", lambda e, p=par: e.tensor_mul(
                    xnt[:, p, :], xnt[:, p, :], rb_s[:, :]))
                b.op("vector", lambda e, kt=kt, p=par: e.tensor_copy(
                    xnh[:, kt, :], xnt[:, p, :]))
                split_done[kt] = b.op("vector", lambda e, kt=kt, p=par: e.tensor_sub(
                    xnl[:, kt, :], xnt[:, p, :], xnh[:, kt, :]), inc=("VE", 1))
                emit_prepass(2)

        def mm3(bank, lhs_hi, lhs_lo, rhs_hi, rhs_lo, start, stop, inc_last=False,
                want_inc=False):
            b.op("tensor", lambda e: e.matmul(ps[:, bank, :], lhs_hi, rhs_hi,
                                              start=start, stop=False))
            b.op("tensor", lambda e: e.matmul(ps[:, bank, :], lhs_hi, rhs_lo,
                                              start=False, stop=False))
            return b.op("tensor", lambda e: e.matmul(ps[:, bank, :], lhs_lo, rhs_hi,
                                                     start=False, stop=stop),
                        inc=("TE", 1) if ((stop and inc_last) or want_inc) else None)

        def attention(j):
            qkv_te = {}
            w_done = {}
            den_te = {}
            recip_done = {}
            bc_te = {}
            attn_split = {}

            def stream_tblock(h, t3):
                slot = (h * 3 + t3) % 2
                kk = scr_done.get(("qkv", h, t3))
                assert kk is not None, f"scratch not ready for qkv head {h} t {t3}"
                b.wait_tok("sync", kk)
                lu = wq_last_use.get(slot)
                if lu is not None:
                    b.wait("sync", "TE", lu)
                v1 = b.dma("sync", lambda e, h=h, t3=t3, s=slot: e.dma_start(
                    wq_b[:, s, 0, :, :].rearrange("p kt c -> p (kt c)"),
                    s_qkv[h, t3, 0, :, :]))
                v2 = b.dma("sync", lambda e, h=h, t3=t3, s=slot: e.dma_start(
                    wq_b[:, s, 1, :, :].rearrange("p kt c -> p (kt c)"),
                    s_qkv[h, t3, 1, :, :]))
                wq_stream[(h, t3)] = ([v1, v2], slot)

            def pe_ttile(h, t3, bank):
                v, slot = wq_stream[(h, t3)]
                b.wait_tok("tensor", v)
                b.guard_bank(bank)
                last = None
                for kt in range(KT):
                    b.wait("tensor", "VE", split_done[kt])
                    last = mm3(bank, wq_b[:, slot, 0, kt, :], wq_b[:, slot, 1, kt, :],
                               xnh[:, kt, :], xnl[:, kt, :],
                               start=(kt == 0), stop=(kt == KT - 1), inc_last=True)
                wq_last_use[slot] = last
                return last

            stream_tblock(0, 0)
            stream_tblock(0, 1)

            for it in range(H + 2):
                h = it
                emit_prepass(6)
                if h < H:
                    p3 = 3 * (h % 2)
                    teq = pe_ttile(h, 0, p3 + 0)
                    stream_tblock(h, 2)
                    tek = pe_ttile(h, 1, p3 + 1)
                if 0 <= h - 1 < H:
                    hp = h - 1
                    b.wait("tensor", "VE", w_done[hp])
                    b.guard_bank(6)
                    den_te[hp] = b.op("tensor", lambda e: e.matmul(
                        ps[0:1, 6, :], ones_c[:, :], w_s[:, :], start=True, stop=True),
                        inc=("TE", 1))
                if h < H:
                    tev = pe_ttile(h, 2, p3 + 2)
                    if h + 1 < H:
                        stream_tblock(h + 1, 0)
                        stream_tblock(h + 1, 1)
                    qkv_te[h] = (teq, tek, tev)
                if 0 <= h - 2 < H:
                    hp = h - 2
                    b.wait("tensor", "VE", recip_done[hp])
                    b.guard_bank(7)
                    bc_te[hp] = b.op("tensor", lambda e, hp=hp: e.matmul(
                        ps[:, 7, :], ones_r[:, :], rden_s[0:1, hp % 2, :],
                        start=True, stop=True), inc=("TE", 1))

                if 0 <= h - 1 < H:
                    hp = h - 1
                    b.wait("vector", "TE", den_te[hp])
                    if hp - 2 >= 0:
                        b.wait("vector", "TE", bc_te[hp - 2])
                    b.op("vector", lambda e: e.tensor_scalar_max(
                        stats[0:1, 4, :], ps[0:1, 6, :], EPS_DEN))
                    cr = b.op("vector", lambda e, hp=hp: e.reciprocal(
                        rden_s[0:1, hp % 2, :], stats[0:1, 4, :]), inc=("VE", 1))
                    b.set_guard(6, "VE", cr)
                    recip_done[hp] = cr

                if 0 <= h - 2 < H:
                    hp = h - 2
                    b.wait("vector", "TE", bc_te[hp])
                    b.op("vector", lambda e, hp=hp: e.tensor_mul(
                        of_s[:, :], u_s[:, hp % 2, :], ps[:, 7, :]))
                    ord_ = b.op("vector", lambda e, hp=hp: e.tensor_copy(
                        ath[:, hp, :], of_s[:, :]), inc=("VE", 1))
                    b.set_guard(7, "VE", ord_)
                    attn_split[hp] = b.op("vector", lambda e, hp=hp: e.tensor_sub(
                        atl[:, hp, :], of_s[:, :], ath[:, hp, :]), inc=("VE", 1))

                if h < H:
                    teq, tek, tev = qkv_te[h]
                    p3 = 3 * (h % 2)
                    b.wait("vector", "TE", teq)
                    ve_minq = b.op("vector", lambda e, p3=p3: e.tensor_scalar_min(
                        sq[:, 1, :], ps[:, p3 + 0, :], 0.0), inc=("VE", 1))
                    b.wait("scalar", "VE", ve_minq)
                    se_eq = b.op("scalar", lambda e: e.activation(
                        ex_s[:, :], sq[:, 1, :], AF.Exp), inc=("SE", 1))
                    b.wait("vector", "SE", se_eq)
                    ve_q = b.op("vector", lambda e, p3=p3: e.scalar_tensor_tensor(
                        q_s[:, :], ps[:, p3 + 0, :], 0.0, ex_s[:, :], ALU.max, ALU.add),
                        inc=("VE", 1))
                    b.set_guard(p3 + 0, "VE", ve_q)
                    b.wait("vector", "TE", tek)
                    ve_mink = b.op("vector", lambda e, p3=p3: e.tensor_scalar_min(
                        sq[:, 1, :], ps[:, p3 + 1, :], 0.0), inc=("VE", 1))
                    b.wait("scalar", "VE", ve_mink)
                    se_ek = b.op("scalar", lambda e: e.activation(
                        ex_s[:, :], sq[:, 1, :], AF.Exp), inc=("SE", 1))
                    b.wait("vector", "SE", se_ek)
                    ve_k = b.op("vector", lambda e, p3=p3: e.scalar_tensor_tensor(
                        k_s[:, :], ps[:, p3 + 1, :], 0.0, ex_s[:, :], ALU.max, ALU.add),
                        inc=("VE", 1))
                    b.set_guard(p3 + 1, "VE", ve_k)
                    b.wait("vector", "TE", tev)
                    ve_kv = b.op("vector", lambda e, p3=p3: e.tensor_mul(
                        kv_s[:, :], k_s[:, :], ps[:, p3 + 2, :]), inc=("VE", 1))
                    b.set_guard(p3 + 2, "VE", ve_kv)
                    b.op("vector", lambda e, h=h: e.tensor_tensor_scan(
                        rkv_s[:, :], dec_s[:, h:h + 1].broadcast_to([128, TT]), kv_s[:, :],
                        ckv_s[:, h:h + 1], ALU.mult, ALU.add))
                    b.op("vector", lambda e, h=h: e.tensor_tensor_scan(
                        rks_s[:, :], dec_s[:, h:h + 1].broadcast_to([128, TT]), k_s[:, :],
                        cks_s[:, h:h + 1], ALU.mult, ALU.add))
                    b.op("vector", lambda e, h=h: e.tensor_mul(
                        u_s[:, h % 2, :], q_s[:, :], rkv_s[:, :]))
                    b.op("vector", lambda e, h=h: e.tensor_copy(
                        ckv_s[:, h:h + 1], rkv_s[:, TT - 1:TT]))
                    b.op("vector", lambda e, h=h: e.tensor_copy(
                        cks_s[:, h:h + 1], rks_s[:, TT - 1:TT]))
                    w_done[h] = b.op("vector", lambda e: e.tensor_mul(
                        w_s[:, :], q_s[:, :], rks_s[:, :]), inc=("VE", 1))
                    if debug and j == dbg_tile and h == 0:
                        b.wait("gpsimd", "VE", w_done[h])
                        caps = []
                        for nm, sb in (("q0", q_s), ("k0", k_s), ("kv0", kv_s),
                                       ("rkv0", rkv_s), ("rks0", rks_s)):
                            caps.append(b.dma("gpsimd", lambda e, nm=nm, sb=sb: e.dma_start(
                                dbg[nm], sb[:, :])))
                        b.wait_tok("vector", caps)

            return attn_split[H - 1]

        def out_proj(j, attn_last):
            x2d = [None] * KT
            for m in range(16):
                slot = m % 2
                kk = scr_done.get(("out", m))
                assert kk is not None
                b.wait_tok("sync", kk)
                lu = wq_last_use.get(slot)
                if lu is not None:
                    b.wait("sync", "TE", lu)
                v1 = b.dma("sync", lambda e, m=m, s=slot: e.dma_start(
                    wo_b[:, s, 0, :, :].rearrange("p kt c -> p (kt c)"), s_out[m, 0, :, :]))
                v2 = b.dma("sync", lambda e, m=m, s=slot: e.dma_start(
                    wo_b[:, s, 1, :, :].rearrange("p kt c -> p (kt c)"), s_out[m, 1, :, :]))
                bank = m % 4
                b.wait_tok("tensor", [v1, v2])
                if m == 0:
                    b.wait("tensor", "VE", attn_last)
                b.guard_bank(bank)
                last = None
                for kt in range(KT):
                    last = mm3(bank, wo_b[:, slot, 0, kt, :], wo_b[:, slot, 1, kt, :],
                               ath[:, kt, :], atl[:, kt, :],
                               start=(kt == 0), stop=(kt == KT - 1), inc_last=True)
                wq_last_use[slot] = last
                b.wait("vector", "TE", last)
                ve = b.op("vector", lambda e, m=m, bank=bank: e.tensor_add(
                    xm[:, m, :], xm[:, m, :], ps[:, bank, :]), inc=("VE", 1))
                b.set_guard(bank, "VE", ve)
                x2d[m] = ve
                emit_prepass(2)
            if debug and j == dbg_tile:
                b.wait("gpsimd", "VE", x2d[15])
                c1 = b.dma("gpsimd", lambda e: e.dma_start(dbg["x2"], xm[:, :, :]))
                c2 = b.dma("gpsimd", lambda e: e.dma_start(dbg["att_hi"], ath[:, :, :]))
                b.wait_tok("vector", [c1, c2])
            return x2d

        def ffn(j):
            last_acc = None
            prev_h_done = [None] * 8
            def slab_guard(slot):
                if SAFE_SLAB:
                    vals = [v for v in wf_last_use.values() if v is not None]
                    return max(vals) if vals else None
                return wf_last_use.get(slot)
            for hc in range(8):
                emit_prepass(48)
                gate_done = [None] * 8
                for kd in range(KT):
                    slot = kd % 2
                    kk = scr_done.get(("gate", hc, kd))
                    assert kk is not None
                    b.wait_tok("sync", kk)
                    lu = slab_guard(slot)
                    if lu is not None:
                        b.wait("sync", "TE", lu)
                    v1 = b.dma("sync", lambda e, hc=hc, kd=kd, s=slot: e.dma_start(
                        wf_b[:, s, 0, :], s_gate[hc, kd, 0, :, :]))
                    v2 = b.dma("sync", lambda e, hc=hc, kd=kd, s=slot: e.dma_start(
                        wf_b[:, s, 1, :], s_gate[hc, kd, 1, :, :]))
                    b.wait_tok("tensor", [v1, v2])
                    b.wait("tensor", "VE", split_done[kd])
                    last = None
                    for m in range(8):
                        if kd == 0:
                            b.guard_bank(m)
                        last = mm3(m, wf_b[:, slot, 0, m * 128:(m + 1) * 128],
                                   wf_b[:, slot, 1, m * 128:(m + 1) * 128],
                                   xnh[:, kd, :], xnl[:, kd, :],
                                   start=(kd == 0), stop=(kd == KT - 1), inc_last=True,
                                   want_inc=(m == 7))
                        if kd == KT - 1:
                            gate_done[m] = last
                    wf_last_use[slot] = last
                silu_done = [None] * 8
                for m in range(8):
                    b.wait("scalar", "TE", gate_done[m])
                    if prev_h_done[m] is not None:
                        b.wait("scalar", "VE", prev_h_done[m])
                    silu_done[m] = b.op("scalar", lambda e, m=m: e.activation(
                        gate_s[:, m, :], ps[:, m, :], AF.Silu), inc=("SE", 1))
                if debug and j == dbg_tile:
                    b.wait("gpsimd", "SE", silu_done[7])
                    cg = b.dma("gpsimd", lambda e, hc=hc: e.dma_start(
                        dbg["gate0"][hc:hc + 1], gate_s[:, :, :]))
                    b.wait_tok("scalar", cg)
                up_done = [None] * 8
                for kd in range(KT):
                    slot = kd % 2
                    kk = scr_done.get(("up", hc, kd))
                    b.wait_tok("sync", kk)
                    lu = slab_guard(slot)
                    if lu is not None:
                        b.wait("sync", "TE", lu)
                    v1 = b.dma("sync", lambda e, hc=hc, kd=kd, s=slot: e.dma_start(
                        wf_b[:, s, 0, :], (s_gate if UP_USES_GATE else s_up)[hc, kd, 0, :, :]))
                    v2 = b.dma("sync", lambda e, hc=hc, kd=kd, s=slot: e.dma_start(
                        wf_b[:, s, 1, :], (s_gate if UP_USES_GATE else s_up)[hc, kd, 1, :, :]))
                    b.wait_tok("tensor", [v1, v2])
                    last = None
                    for m in range(8):
                        if kd == 0:
                            b.wait("tensor", "SE", silu_done[m])
                        last = mm3(m, wf_b[:, slot, 0, m * 128:(m + 1) * 128],
                                   wf_b[:, slot, 1, m * 128:(m + 1) * 128],
                                   xnh[:, kd, :], xnl[:, kd, :],
                                   start=(kd == 0), stop=(kd == KT - 1), inc_last=True,
                                   want_inc=(m == 7))
                        if kd == KT - 1:
                            up_done[m] = last
                    wf_last_use[slot] = last
                h_done = [None] * 8
                for m in range(8):
                    b.wait("vector", "TE", up_done[m])
                    if debug and j == dbg_tile and hc == 0:
                        b.op("vector", lambda e, m=m: e.tensor_copy(
                            updbg[:, m, :], ps[:, m, 0:TT // 2]))
                    b.op("vector", lambda e, m=m: e.tensor_mul(
                        hf_s[:, :], gate_s[:, m, :], ps[:, m, :]))
                    b.op("vector", lambda e, m=m: e.tensor_copy(ath[:, m, :], hf_s[:, :]))
                    ve = b.op("vector", lambda e, m=m: e.tensor_sub(
                        atl[:, m, :], hf_s[:, :], ath[:, m, :]), inc=("VE", 1))
                    b.set_guard(m, "VE", ve)
                    h_done[m] = ve
                    prev_h_done[m] = ve
                if debug and j == dbg_tile:
                    b.wait("gpsimd", "VE", h_done[7])
                    ch1 = b.dma("gpsimd", lambda e, hc=hc: e.dma_start(
                        dbg["h0"][hc:hc + 1], ath[:, 0:8, :]))
                    ch2 = b.dma("gpsimd", lambda e, hc=hc: e.dma_start(
                        dbg["h0lo"][hc:hc + 1], atl[:, 0:8, :]))
                    caps = [ch1, ch2]
                    if hc == 0:
                        caps.append(b.dma("gpsimd", lambda e: e.dma_start(
                            dbg["up0"], updbg[:, :, :])))
                    b.wait_tok("vector", caps)
                for dh in range(2):
                    down_done = [None] * 8
                    for kh in range(8):
                        slot = kh % 2
                        kk = scr_done.get(("down", hc, dh, kh))
                        b.wait_tok("sync", kk)
                        lu = slab_guard(slot)
                        if lu is not None:
                            b.wait("sync", "TE", lu)
                        v1 = b.dma("sync", lambda e, hc=hc, dh=dh, kh=kh, s=slot: e.dma_start(
                            wf_b[:, s, 0, :], s_down[hc, dh, kh, 0, :, :]))
                        v2 = b.dma("sync", lambda e, hc=hc, dh=dh, kh=kh, s=slot: e.dma_start(
                            wf_b[:, s, 1, :], s_down[hc, dh, kh, 1, :, :]))
                        b.wait_tok("tensor", [v1, v2])
                        b.wait("tensor", "VE", h_done[kh])
                        last = None
                        for m in range(8):
                            if kh == 0:
                                b.guard_bank(m)
                            last = mm3(m, wf_b[:, slot, 0, m * 128:(m + 1) * 128],
                                       wf_b[:, slot, 1, m * 128:(m + 1) * 128],
                                       ath[:, kh, :], atl[:, kh, :],
                                       start=(kh == 0), stop=(kh == 7), inc_last=True,
                                       want_inc=(m == 7))
                            if kh == 7:
                                down_done[m] = last
                        wf_last_use[slot] = last
                    for m in range(8):
                        dt_ = dh * 8 + m
                        b.wait("vector", "TE", down_done[m])
                        ve = b.op("vector", lambda e, m=m, dt_=dt_: e.tensor_add(
                            xm[:, dt_, :], xm[:, dt_, :], ps[:, m, :]), inc=("VE", 1))
                        b.set_guard(m, "VE", ve)
                        last_acc = ve
            return last_acc

        for j in range(NTILE):
            ln_phase(j, "ln1")
            attn_last = attention(j)
            if debug and j == dbg_tile:
                b.wait("gpsimd", "VE", split_done[15])
                cx = b.dma("gpsimd", lambda e: e.dma_start(dbg["xn_hi"], xnh[:, :, :]))
                b.wait_tok("vector", cx)
            x2_done_ref[0] = out_proj(j, attn_last)
            ln_phase(j, "ln2")
            if debug and j == dbg_tile:
                b.wait("gpsimd", "VE", split_done[15])
                cx2 = b.dma("gpsimd", lambda e: e.dma_start(dbg["xn2_hi"], xnh[:, :, :]))
                b.wait_tok("vector", cx2)
            last_acc = ffn(j)
            b.wait("gpsimd", "VE", last_acc)
            ys = b.dma("gpsimd", lambda e, j=j: e.dma_start(
                y.rearrange("(kt p) (jj t) -> p kt jj t", p=128, t=TT)[:, :, j, :],
                xm[:, :, :]))
            if j + 1 < NTILE:
                b.wait_tok("sync", ys)
                x_loaded[j + 1] = b.dma("sync", lambda e, j=j: e.dma_start(
                    xm[:, :, :],
                    xT.rearrange("(kt p) (jj t) -> p kt jj t", p=128, t=TT)[:, :, j + 1, :]))

        emit_prepass(len(chunks))
        assert pp["split_i"] == len(chunks), f"prepass incomplete {pp['split_i']}/{len(chunks)}"
        if debug:
            for pn in Builder.POOLS["scalar"]:
                b.wait("gpsimd", pn, b.cnt.get(pn, 0))
            b.dma("gpsimd", lambda e: e.dma_start(dbg["sgate"], s_gate))
        for pn in Builder.POOLS["gpsimd"]:
            b.wait("sync", pn, b.cnt.get(pn, 0))

        with nc.Block() as block:
            @block.sync
            def _(sync):
                for f in b.ops["sync"]:
                    f(sync)

            @block.scalar
            def _(scalar):
                for f in b.ops["scalar"]:
                    f(scalar)

            @block.vector
            def _(vector):
                for f in b.ops["vector"]:
                    f(vector)

            @block.tensor
            def _(tensor):
                for f in b.ops["tensor"]:
                    f(tensor)

            @block.gpsimd
            def _(gpsimd):
                for f in b.ops["gpsimd"]:
                    f(gpsimd)

    return nc


def stage_weights(w_qkv, w_out, w_gate, w_up, w_down, decay):
    wq = np.asarray(w_qkv, dtype=np.float32)
    wo = np.asarray(w_out, dtype=np.float32)
    wg = np.asarray(w_gate, dtype=np.float32)
    wu = np.asarray(w_up, dtype=np.float32)
    wd = np.asarray(w_down, dtype=np.float32)
    qkvW = np.ascontiguousarray(
        wq.reshape(KT, 128, 3, H, 128).transpose(3, 2, 1, 0, 4).reshape(H, 3, 128, KT * 128))
    outW = np.ascontiguousarray(
        wo.reshape(KT, 128, 16, 128).transpose(2, 1, 0, 3).reshape(16, 128, KT * 128))
    gateW = np.ascontiguousarray(wg.reshape(8, 1024, KT, 128).transpose(0, 2, 3, 1))
    upW = np.ascontiguousarray(wu.reshape(8, 1024, KT, 128).transpose(0, 2, 3, 1))
    downW = np.ascontiguousarray(wd.reshape(2, 1024, 8, 8, 128).transpose(2, 0, 3, 4, 1))
    dkv = np.ascontiguousarray(np.asarray(decay, dtype=np.float32).reshape(1, H))
    return qkvW, outW, gateW, upW, downW, dkv


def kernel(L0_attn_x, L0_attn_w_qkv, L0_attn_w_out, L0_attn_decay_param,
           L0_ffn_w_gate, L0_ffn_w_up, L0_ffn_w_down):
    x = np.asarray(L0_attn_x, dtype=np.float32)
    qkvW, outW, gateW, upW, downW, dkv = stage_weights(
        L0_attn_w_qkv, L0_attn_w_out, L0_ffn_w_gate, L0_ffn_w_up,
        L0_ffn_w_down, L0_attn_decay_param)
    nc = build_program(debug=False)
    in_maps = []
    for b_ in range(NCORES):
        in_maps.append({
            "xT": np.ascontiguousarray(x[b_].T),
            "qkvW": qkvW, "outW": outW, "gateW": gateW, "upW": upW,
            "downW": downW, "dk": dkv,
        })
    res = run_bass_kernel_spmd(nc, in_maps, core_ids=list(range(NCORES)))
    out = np.empty((NCORES, T, DM), dtype=np.float32)
    for b_ in range(NCORES):
        out[b_] = res.results[b_]["y"].T
    return out
